# version 03
# baseline (speedup 1.0000x reference)
import numpy as np
import ml_dtypes
from contextlib import ExitStack

import concourse.bass as bass
import concourse.mybir as mybir
from concourse.bass_utils import run_bass_kernel_spmd

F32 = mybir.dt.float32
BF16 = mybir.dt.bfloat16
I32 = mybir.dt.int32
AF = mybir.ActivationFunctionType
AO = mybir.AluOpType

ALPHA = 0.99
B, S, D = 2, 8192, 1024
T = 2048
C = 512
NSC = T // C
NF = D // 128
NTT = T // 128
R = 32
EPS = 1e-5
NCORES = 8

DEBUG_TAPS = []


class Sem:
    def __init__(self, ctx, nc, name):
        self.h = ctx.enter_context(nc.semaphore(name))
        self.n = 0

    def inc(self, k=1):
        self.n += k
        return self.n


class Waits:
    def __init__(self):
        self.hi = {}

    def need(self, eng_key, sem, val):
        if val <= 0:
            return False
        key = (eng_key, sem.h.num)
        if self.hi.get(key, 0) >= val:
            return False
        self.hi[key] = val
        return True


class Rot:
    def __init__(self, banks):
        self.banks = banks
        self.free_at = [None] * len(banks)
        self.i = 0

    def acquire(self):
        b = self.i % len(self.banks)
        self.i += 1
        return b, self.banks[b], self.free_at[b]

    def release(self, b, sem, count):
        self.free_at[b] = (sem, count)


def _nbytes(shape, dt):
    n = 1
    for s in shape[1:]:
        n *= s
    return n * mybir.dt.size(dt)


def build_program(upto="full", debug=False):
    nc = bass.Bass("TRN2", target_bir_lowering=False)

    def inp(name, shape, dt):
        return nc.dram_tensor(name, shape, dt, kind="ExternalInput").ap()

    x_h = inp("x", [T, D], F32)
    wq_h = inp("wqkvT", [D, 3 * D], F32)
    bq_h = inp("bqkv", [1, 3 * D], F32)
    qw_h = inp("qwT", [6, D, D], I32)
    sr_h = inp("srepT", [6, D, D], F32)
    bias_h = inp("biases", [6, 1024], F32)
    la_h = inp("laT_r", [128, 6 * NF * R], F32)
    lb_h = inp("lbT", [6, R, D], F32)
    gam_h = inp("gamma_fm", [128, 3, NF], F32)
    bet_h = inp("beta_fm", [128, 3, NF], F32)
    cw_h = inp("chainw", [128, 4], F32)
    id_h = inp("ident", [128, 128], BF16)
    on_h = inp("ones_bf", [1, 512], BF16)
    cz_h = inp("causal", [128, 4, C], BF16)
    qd_h = inp("qdecay", [128, T], F32)
    kd_h = inp("kdecay", [128, NTT], F32)
    out_h = nc.dram_tensor("out", [T, D], F32, kind="ExternalOutput").ap()
    agin = nc.dram_tensor("agin", [D, D], BF16, kind="Internal")
    agout = nc.dram_tensor("agout", [4 * D, D], BF16, kind="Internal")
    taps_h = {}
    if debug:
        for nm, shp, dt in (("tap_hx", [NTT * 128, D], BF16),
                            ("tap_qdec", [NF * 128, T], BF16),
                            ("tap_U", [NF * 128, D], F32),
                            ("tap_ktm", [4 * 128, D], BF16),
                            ("tap_vtm", [4 * 128, D], BF16),
                            ("tap_sT", [4 * 128, C], BF16),
                            ("tap_WQ", [NF * 128, 3 * D], BF16),
                            ("tap_sbf", [NF * 128, D], BF16),
                            ("tap_h", [NTT * 128, D], BF16),
                            ("tap_w0", [NF * 128, D], BF16),
                            ("tap_y1", [4 * 128, D], BF16),
                            ("tap_lnsc", [128, 576], F32),
                            ("tap_sum", [128, 1024], F32),
                            ("tap_lnsc2", [128, 32], F32),
                            ("tap_st6", [128, 192], F32),
                            ("tap_mv4", [128, 64], F32)):
            taps_h[nm] = nc.dram_tensor(nm, shp, dt, kind="ExternalOutput").ap()

    A0 = 20480
    TOP = 229344

    def sb(name, shape, dt, off):
        assert A0 + off + _nbytes(shape, dt) <= TOP, f"{name} overflows"
        return nc.alloc_sbuf_tensor_at(name, shape, dt, offset=A0 + off)

    hx = sb("hx", [128, NTT, D], BF16, 0)
    WQ = sb("WQ", [128, NF, 3 * D], BF16, 32768)
    hF = sb("hF", [128, NTT, D], BF16, 32768)
    stq = sb("stq", [128, 2, D], I32, 65536)
    sts = sb("sts", [128, 2, D], F32, 73728)
    lbst = sb("lbst", [R, 2, D], F32, 65536)
    lbbf = sb("lbbf", [R, 2, D], BF16, 73728)
    U = sb("U", [128, NF, D], F32, 81920)
    qdec = sb("qdec", [128, NF, T], BF16, 114688)
    xstg = sb("xstg", [128, 2, 4, D], F32, 81920)
    wstg = sb("wstg", [128, 2, 3 * D], F32, 114688)
    wslot = [sb(f"wslot{i}", [128, NF, D], BF16, 81920 + 16384 * i)
             for i in range(3)]
    t1bf = sb("t1bf", [R, 2, C], BF16, 131072)
    b2row = sb("b2row", [1, D], BF16, 133120)
    sumall = sb("sumall", [128, NTT, D], F32, 81920)
    sumalt = sb("sumalt", [128, 2, D], F32, 172032)
    xfm = sb("xfm", [128, NF, C], BF16, 147456)
    ubf = sb("ubf", [128, NF, 512], BF16, 147456)
    ktm = sb("ktm", [128, 4, D], BF16, 155648)
    sT = sb("sT", [128, 4, C], BF16, 155648)
    kfm = sb("kfm", [128, NF, C], BF16, 163840)
    vtm = sb("vtm", [128, 4, D], BF16, 172032)
    sbf = [sb("sbfA", [128, NF, 512], BF16, 147456),
           sb("sbfB", [128, NF, 512], BF16, 155648)]
    agu = sb("agu", [128, NF, 512], BF16, 163840)
    gwin = [sb("gwinA", [128, NF, 512], BF16, 163840),
            sb("gwinB", [128, NF, 512], BF16, 172032)]
    hfm = sb("hfm", [128, NF, 512], BF16, 147456)
    yfm = sb("yfm", [128, NF, 512], BF16, 155648)
    y1bf = sb("y1bf", [128, 4, D], BF16, 163840)
    tmpbf = sb("tmpbf", [128, 4, D], BF16, 172032)
    out32 = sb("out32", [128, 2, D], F32, 147456)
    ln32 = sb("ln32", [128, 2, D], F32, 147456)
    co = [180224]

    def cst(name, shape, dt):
        co[0] = (co[0] + 31) & ~31
        t = sb(name, shape, dt, co[0])
        co[0] += _nbytes(shape, dt)
        return t

    ident = cst("identc", [128, 128], BF16)
    ones = cst("onesc", [1, 512], BF16)
    bqkv = cst("bqkvc", [1, 3 * D], BF16)
    causal = cst("causalc", [128, 4, C], BF16)
    qdk = cst("qdkc", [128, T], F32)
    biasbf = sb("biasbfc", [1, 6 * 1024], BF16, co[0] - 12288)
    kdk = cst("kdkc", [128, NTT], F32)
    chw = cst("chwc", [128, 4], F32)
    gfm = cst("gfmc", [128, 3, NF], F32)
    bfm = cst("bfmc", [128, 3, NF], F32)
    bfmb = cst("bfmbc", [128, 3, NF], BF16)
    labf = cst("labfc", [128, 6 * NF * R], BF16)
    lnsc = sb("lnscc", [128, 640], F32, 138240)
    magici = cst("magicic", [128, 64], I32)
    dumv = cst("dumvc", [128, 64], F32)
    st6 = sb("st6c", [128, 64, 12], F32, 135168)
    dqtmp = cst("dqtmpc", [128, 2, D], BF16, )
    assert co[0] <= TOP - A0, f"consts overflow: {co[0]}"

    with ExitStack() as ctx:
        PSF = [ctx.enter_context(nc.psum_tensor(f"psf{i}", [128, 512], F32))
               for i in range(6)]
        PTB = [ctx.enter_context(nc.psum_tensor(f"ptb{i}", [128, 512], BF16))
               for i in range(2)]
        s_c = Sem(ctx, nc, "s_c")
        s_x2 = [Sem(ctx, nc, "s_xa"), Sem(ctx, nc, "s_xb")]
        s_w2 = [Sem(ctx, nc, "s_wa"), Sem(ctx, nc, "s_wb")]
        s_q2 = [Sem(ctx, nc, "s_qa"), Sem(ctx, nc, "s_qb")]
        s_lb = Sem(ctx, nc, "s_lb")
        s_bs = Sem(ctx, nc, "s_bs")
        s_g2 = [Sem(ctx, nc, "s_ga"), Sem(ctx, nc, "s_gb")]
        s_gi = Sem(ctx, nc, "s_gi")
        s_cc = Sem(ctx, nc, "s_cc")
        s_mm = Sem(ctx, nc, "s_mm")
        s_ev = Sem(ctx, nc, "s_ev")
        s_se = Sem(ctx, nc, "s_se")
        s_st2 = [Sem(ctx, nc, "s_sta"), Sem(ctx, nc, "s_stb")]
        block = ctx.enter_context(nc.Block())

        W = Waits()
        rot = Rot(PSF)
        rott = Rot(PTB)
        ops = []
        all_ops = []

        def op(eng, waits, emit, incs=()):
            w = [(s, v) for (s, v) in waits if v > 0]
            ops.append((eng, w, emit))
            all_ops.append((eng, w, list(incs)))

        def flush():
            nonlocal ops
            local, ops = ops, []
            for eng in ("sync", "gpsimd", "tensor", "vector", "scalar"):
                mine = [o for o in local if o[0] == eng]
                if not mine:
                    continue

                def body(e, mine=mine, eng=eng):
                    for _, waits, emit in mine:
                        for sem, val in waits:
                            if W.need(eng, sem, val):
                                e.wait_ge(sem.h, val)
                        emit(e)

                getattr(block, eng)(body)

        def mm_group(waits, emitters):
            banks = []
            allw = list(waits)
            for _ in emitters:
                bi, bt, fa = rot.acquire()
                banks.append((bi, bt))
                if fa is not None:
                    allw.append(fa)
            val = s_mm.inc()

            def emit(t):
                for (bi, bt), fn in zip(banks, emitters):
                    fn(t, bt)
                t.drain().then_inc(s_mm.h, 1)

            op("tensor", allw, emit, [(s_mm, 1)])
            return val, [bi for bi, _ in banks]

        def tr_group(waits, rounds):
            bi, bt, fa = rott.acquire()
            allw = list(waits)
            if fa is not None:
                allw.append(fa)
            val = s_mm.inc()

            def emit(t):
                for in_ap, coff in rounds:
                    t.transpose(bt[:, coff:coff + 128], in_ap, ident[:])
                t.drain().then_inc(s_mm.h, 1)

            op("tensor", allw, emit, [(s_mm, 1)])
            return val, bi, bt

        def emit_taps():
            if not debug:
                return
            qsc = [(s_ev, s_ev.n), (s_se, s_se.n), (s_mm, s_mm.n),
                   (s_c, s_c.n), (s_x2[0], s_x2[0].n), (s_x2[1], s_x2[1].n),
                   (s_w2[0], s_w2[0].n), (s_w2[1], s_w2[1].n),
                   (s_q2[0], s_q2[0].n), (s_q2[1], s_q2[1].n),
                   (s_lb, s_lb.n), (s_bs, s_bs.n),
                   (s_g2[0], s_g2[0].n), (s_g2[1], s_g2[1].n),
                   (s_gi, s_gi.n), (s_cc, s_cc.n)]
            pairs = [('tap_lnsc', lnsc[:, 0:576], 1),
                     ('tap_sum', sumall[:, 0, :], 1),
                     ('tap_st6', st6[:, 0:16, :].rearrange('p a b -> p (a b)'), 1),
                     ('tap_hx', hx[:], NTT), ('tap_qdec', qdec[:], NF),
                     ('tap_U', U[:], NF), ('tap_ktm', ktm[:], 4),
                     ('tap_vtm', vtm[:], 4), ('tap_sT', sT[:], 4),
                     ('tap_WQ', WQ[:], NF), ('tap_sbf', None, NF),
                     ('tap_h', hF[:], NTT), ('tap_w0', wslot[0][:], NF),
                     ('tap_y1', y1bf[:], 4)]
            for nm, src, ntile in pairs:
                if nm not in taps_h:
                    continue
                if nm == 'tap_sbf':
                    def te(sy, nm=nm):
                        d0 = taps_h[nm].rearrange('(a p) d -> p a d', p=128)
                        sy.dma_start(out=d0[:, :, 0:512], in_=sbf[0][:]).then_inc(s_st2[0].h, 16)
                        sy.dma_start(out=d0[:, :, 512:1024], in_=sbf[1][:]).then_inc(s_st2[0].h, 16)
                    op('sync', qsc, te, [(s_st2[0], 32)])
                    s_st2[0].inc(32)
                    continue
                def te(sy, nm=nm, src=src):
                    dst = taps_h[nm]
                    if dst.shape[0] != 128:
                        dst = dst.rearrange('(a p) d -> p a d', p=128)
                    sy.dma_start(out=dst, in_=src).then_inc(s_st2[0].h, 16)
                op('sync', qsc, te, [(s_st2[0], 16)])
                s_st2[0].inc(16)

        def emit_all():
            def dma(dst, src, sem):
                def emit(e):
                    e.dma_start(out=dst, in_=src).then_inc(sem.h, 16)
                return emit

            for dst, src in ((ident[:], id_h[:]), (ones[:], on_h[:]),
                             (causal[:], cz_h[:]), (qdk[:], qd_h[:]),
                             (kdk[:], kd_h[:]), (chw[:], cw_h[:]),
                             (gfm[:], gam_h[:]), (bet := bfm[:], bet_h[:])):
                op("sync", [], dma(dst, src, s_c), [(s_c, 16)])
                s_c.inc(16)
            w_const = s_c.n

            xcast_ev = [0] * NTT
            for d in range(4):
                src = x_h[d * 512:(d + 1) * 512, :].rearrange("(i p) d -> p i d", p=128)
                buf = d % 2
                waits = [(s_ev, xcast_ev[(d - 2) * 4 + 3])] if d >= 2 else []
                op("sync", waits, dma(xstg[:, buf], src, s_x2[buf]), [(s_x2[buf], 16)])
                s_x2[buf].inc(16)
                xv = s_x2[buf].n

                def cast_emit(v, d=d, buf=buf):
                    for i in range(4):
                        q = v.tensor_copy(hx[:, d * 4 + i, :], xstg[:, buf, i, :])
                    q.then_inc(s_ev.h, 1)

                op("vector", [(s_x2[buf], xv)], cast_emit, [(s_ev, 1)])
                ev = s_ev.inc()
                for i in range(4):
                    xcast_ev[d * 4 + i] = ev

            wcast_se = [0] * NF
            for f in range(NF):
                buf = f % 2
                waits = [(s_se, wcast_se[f - 2])] if f >= 2 else []
                op("sync", waits, dma(wstg[:, buf], wq_h[f * 128:(f + 1) * 128, :], s_w2[buf]), [(s_w2[buf], 16)])
                s_w2[buf].inc(16)
                wv = s_w2[buf].n

                def wc_emit(sc, f=f, buf=buf):
                    sc.copy(WQ[:, f, :], wstg[:, buf, :]).then_inc(s_se.h, 1)

                op("scalar", [(s_w2[buf], wv)], wc_emit, [(s_se, 1)])
                wcast_se[f] = s_se.inc()

            op("sync", [(s_se, wcast_se[NF - 2])], dma(wstg[0:1, 0, :], bq_h[:], s_w2[0]), [(s_w2[0], 16)])
            s_w2[0].inc(16)
            bqv = s_w2[0].n

            def bq_emit(sc):
                sc.copy(bqkv[:], wstg[0:1, 0, :]).then_inc(s_se.h, 1)

            op("scalar", [(s_w2[0], bqv)], bq_emit, [(s_se, 1)])
            se_bq = s_se.inc()

            op("sync", [(s_se, wcast_se[NF - 1])],
               dma(wstg[:, 1, 0:6 * NF * R], la_h[:], s_w2[1]), [(s_w2[1], 16)])
            s_w2[1].inc(16)
            lav = s_w2[1].n

            def la_emit(v):
                v.tensor_copy(labf[:], wstg[:, 1, 0:6 * NF * R]).then_inc(s_ev.h, 1)

            op("vector", [(s_w2[1], lav)], la_emit, [(s_ev, 1)])
            ev_la = s_ev.inc()

            def misc_emit(v):
                v.memset(magici[:], 0x5F3759DF)
                v.tensor_copy(bfmb[:].rearrange("p a b -> p (a b)"),
                              bfm[:].rearrange("p a b -> p (a b)")
                              ).then_inc(s_ev.h, 1)

            op("vector", [(s_c, w_const)], misc_emit, [(s_ev, 1)])
            ev_misc = s_ev.inc()

            flush()
            if upto == 'p0':
                return

            ev_xfm_last = 0
            ev_qdec = 0
            se_ktm = 0
            ev_vtm = 0
            ev_sT = 0
            ev_ubf = [0, 0]
            ev_U = 0
            ev_kfm = 0
            mm_last_attn = 0
            mm_kv_done = 0
            mm_ktm_done = 0
            ev_hx1 = 0

            for c in range(NSC):
                for f in range(NF):
                    waits = [(s_ev, xcast_ev[c * 4 + 3])]
                    if c > 0:
                        waits.append((s_mm, mm_last_attn))
                    rounds = [(hx[:, c * 4 + i, f * 128:(f + 1) * 128], i * 128)
                              for i in range(4)]
                    val, bi, bt = tr_group(waits, rounds)

                    def ev_emit(v, f=f, bt=bt):
                        v.tensor_copy(xfm[:, f, :], bt[:, 0:512]).then_inc(s_ev.h, 1)

                    op("vector", [(s_mm, val)], ev_emit, [(s_ev, 1)])
                    ev_xfm_last = s_ev.inc()
                    rott.release(bi, s_ev, ev_xfm_last)

                for st0 in range(0, 8, 3):
                    qos = list(range(st0, min(st0 + 3, 8)))
                    waits = [(s_ev, ev_xfm_last), (s_se, se_bq)]
                    ems = []
                    for qo in qos:
                        def mk(qo=qo):
                            def mm(t, bt):
                                for f in range(NF):
                                    t.matmul(bt[:], WQ[:, f, qo * 128:(qo + 1) * 128],
                                             xfm[:, f, :], start=(f == 0), stop=False)
                                t.matmul(bt[:], bqkv[0:1, qo * 128:(qo + 1) * 128],
                                         ones[0:1, :], start=False, stop=True)
                            return mm
                        ems.append(mk())
                    val, banks = mm_group(waits, ems)
                    for qo, bi in zip(qos, banks):
                        def ev_emit(v, qo=qo, bi=bi, c=c):
                            v.tensor_tensor(qdec[:, qo, c * C:(c + 1) * C], PSF[bi][:],
                                            qdk[:, c * C:(c + 1) * C], AO.mult
                                            ).then_inc(s_ev.h, 1)

                        w2 = [(s_mm, val)]
                        if c == 0:
                            w2.append((s_ev, ev_la))
                        op("vector", w2, ev_emit, [(s_ev, 1)])
                        ev_qdec = s_ev.inc()
                        rot.release(bi, s_ev, ev_qdec)

                for st0 in range(0, 16, 3):
                    items = [(i // 4, i % 4) for i in range(st0, min(st0 + 3, 16))]
                    waits = [(s_ev, ev_xfm_last), (s_se, se_bq)]
                    ems = []
                    for tt, h in items:
                        def mk(tt=tt, h=h):
                            def mm(t, bt):
                                o0 = D + h * 512
                                for f in range(NF):
                                    t.matmul(bt[:], xfm[:, f, tt * 128:(tt + 1) * 128],
                                             WQ[:, f, o0:o0 + 512],
                                             start=(f == 0), stop=False)
                                t.matmul(bt[:], ones[0:1, 0:128], bqkv[0:1, o0:o0 + 512],
                                         start=False, stop=True)
                            return mm
                        ems.append(mk())
                    val, banks = mm_group(waits, ems)
                    for (tt, h), bi in zip(items, banks):
                        if h < 2:
                            def ev_emit(sc, tt=tt, h=h, bi=bi, c=c):
                                sc.activation(ktm[:, tt, h * 512:(h + 1) * 512],
                                              PSF[bi][:], AF.Copy, bias=0.0,
                                              scale=kdk[:, c * 4 + tt:c * 4 + tt + 1]
                                              ).then_inc(s_se.h, 1)

                            op("scalar", [(s_mm, val)], ev_emit, [(s_se, 1)])
                            se_ktm = s_se.inc()
                            rot.release(bi, s_se, se_ktm)
                        else:
                            def ev_emit(v, tt=tt, h=h, bi=bi):
                                v.tensor_copy(vtm[:, tt, (h - 2) * 512:(h - 1) * 512],
                                              PSF[bi][:]).then_inc(s_ev.h, 1)

                            op("vector", [(s_mm, val)], ev_emit, [(s_ev, 1)])
                            ev_vtm = s_ev.inc()
                            rot.release(bi, s_ev, ev_vtm)
                mm_kv_done = val

                def emit_ubf(dvh, extra_mm):
                    def ub_emit(v, dvh=dvh):
                        for f in range(NF):
                            q = v.tensor_copy(ubf[:, f, :],
                                              U[:, f, dvh * 512:(dvh + 1) * 512])
                        q.then_inc(s_ev.h, 1)

                    op("vector", [(s_mm, extra_mm)], ub_emit, [(s_ev, 1)])
                    return s_ev.inc()

                if c > 0:
                    for dvh in range(2):
                        gate = mm_kv_done if dvh == 0 else mm_x0
                        ev_ubf[dvh] = emit_ubf(dvh, gate)
                        for st0 in range(0, 4, 2):
                            its = list(range(st0, min(st0 + 2, 4)))
                            waits = [(s_ev, ev_ubf[dvh]), (s_ev, ev_qdec)]
                            ems = []
                            for it in its:
                                def mk(it=it, dvh=dvh, c=c):
                                    def mm(t, bt):
                                        for f in range(NF):
                                            t.matmul(
                                                bt[:],
                                                qdec[:, f, c * C + it * 128:
                                                     c * C + (it + 1) * 128],
                                                ubf[:, f, :],
                                                start=(f == 0), stop=(f == NF - 1))
                                    return mm
                                ems.append(mk())
                            val, banks = mm_group(waits, ems)
                            for it, bi in zip(its, banks):
                                def ev_emit(v, it=it, dvh=dvh, bi=bi, c=c):
                                    dst = hx[:, c * 4 + it,
                                             dvh * 512:(dvh + 1) * 512]
                                    v.tensor_add(dst, PSF[bi][:], dst
                                                 ).then_inc(s_ev.h, 1)

                                op("vector", [(s_mm, val)], ev_emit, [(s_ev, 1)])
                                ev_hx1 = s_ev.inc()
                                rot.release(bi, s_ev, ev_hx1)
                        mm_x0 = val

                for dvh in range(2):
                    for st0 in range(0, 8, 3):
                        fs = list(range(st0, min(st0 + 3, 8)))
                        waits = [(s_se, se_ktm), (s_ev, ev_vtm)]
                        ems = []
                        for f in fs:
                            def mk(f=f, dvh=dvh):
                                def mm(t, bt):
                                    for tt in range(4):
                                        t.matmul(bt[:], ktm[:, tt, f * 128:(f + 1) * 128],
                                                 vtm[:, tt, dvh * 512:(dvh + 1) * 512],
                                                 start=(tt == 0), stop=(tt == 3))
                                return mm
                            ems.append(mk())
                        val, banks = mm_group(waits, ems)
                        for f, bi in zip(fs, banks):
                            def ev_emit(v, f=f, dvh=dvh, bi=bi, c=c):
                                dst = U[:, f, dvh * 512:(dvh + 1) * 512]
                                if c == 0:
                                    v.tensor_copy(dst, PSF[bi][:]).then_inc(s_ev.h, 1)
                                else:
                                    v.tensor_add(dst, PSF[bi][:], dst
                                                 ).then_inc(s_ev.h, 1)

                            op("vector", [(s_mm, val)], ev_emit, [(s_ev, 1)])
                            ev_U = s_ev.inc()
                            rot.release(bi, s_ev, ev_U)

                for f in range(NF):
                    rounds = [(ktm[:, i, f * 128:(f + 1) * 128], i * 128)
                              for i in range(4)]
                    val, bi, bt = tr_group([(s_se, se_ktm)], rounds)

                    def ev_emit(v, f=f, bt=bt):
                        v.tensor_copy(kfm[:, f, :], bt[:, 0:512]).then_inc(s_ev.h, 1)

                    op("vector", [(s_mm, val)], ev_emit, [(s_ev, 1)])
                    ev_kfm = s_ev.inc()
                    rott.release(bi, s_ev, ev_kfm)
                mm_ktm_done = val

                for st0 in range(0, 4, 2):
                    jts = list(range(st0, min(st0 + 2, 4)))
                    waits = [(s_ev, ev_kfm), (s_ev, ev_qdec)]
                    ems = []
                    for jt in jts:
                        def mk(jt=jt, c=c):
                            def mm(t, bt):
                                for f in range(NF):
                                    t.matmul(bt[:], kfm[:, f, jt * 128:(jt + 1) * 128],
                                             qdec[:, f, c * C:(c + 1) * C],
                                             start=(f == 0), stop=(f == NF - 1))
                            return mm
                        ems.append(mk())
                    val, banks = mm_group(waits, ems)
                    for jt, bi in zip(jts, banks):
                        def ev_emit(v, jt=jt, bi=bi):
                            v.tensor_tensor(sT[:, jt, :], PSF[bi][:], causal[:, jt, :],
                                            AO.mult).then_inc(s_ev.h, 1)

                        op("vector", [(s_mm, max(val, mm_ktm_done))], ev_emit, [(s_ev, 1)])
                        ev_sT = s_ev.inc()
                        rot.release(bi, s_ev, ev_sT)

                for dvh in range(2):
                    for st0 in range(0, 4, 2):
                        its = list(range(st0, min(st0 + 2, 4)))
                        waits = [(s_ev, ev_sT), (s_ev, ev_vtm)]
                        ems = []
                        for it in its:
                            def mk(it=it, dvh=dvh):
                                def mm(t, bt):
                                    for jt in range(4):
                                        t.matmul(bt[:], sT[:, jt, it * 128:(it + 1) * 128],
                                                 vtm[:, jt, dvh * 512:(dvh + 1) * 512],
                                                 start=(jt == 0),
                                                 stop=(jt == 3))
                                return mm
                            ems.append(mk())
                        val, banks = mm_group(waits, ems)
                        for it, bi in zip(its, banks):
                            def ev_emit(v, it=it, dvh=dvh, bi=bi, c=c):
                                dst = hx[:, c * 4 + it, dvh * 512:(dvh + 1) * 512]
                                v.tensor_add(dst, PSF[bi][:], dst).then_inc(s_ev.h, 1)

                            op("vector", [(s_mm, val)], ev_emit, [(s_ev, 1)])
                            ev_hx1 = s_ev.inc()
                            rot.release(bi, s_ev, ev_hx1)
                mm_last_attn = val

                flush()
                if upto == f'attn{c}':
                    return

            mm_attn_done = s_mm.n

            gg_half = [0, 0]
            ev_agu = 0
            for dvh in range(2):
                def cast_emit(v, dvh=dvh):
                    for f in range(NF):
                        q = v.tensor_copy(agu[:, f, :],
                                          U[:, f, dvh * 512:(dvh + 1) * 512])
                    q.then_inc(s_ev.h, 1)

                waits = [(s_ev, ev_U), (s_mm, mm_attn_done)]
                if dvh == 1:
                    waits.append((s_gi, gg_half[0]))
                op("vector", waits, cast_emit, [(s_ev, 1)])
                ev_agu = s_ev.inc()

                def dma_emit(g, dvh=dvh, ev=ev_agu):
                    dst = agin.ap().rearrange("(f p) dv -> p f dv", p=128)
                    g.dma_start(out=dst[:, :, dvh * 512:(dvh + 1) * 512],
                                in_=agu[:]).then_inc(s_gi.h, 16)

                op("gpsimd", [(s_ev, ev_agu)], dma_emit, [(s_gi, 16)])
                s_gi.inc(16)
                gg_half[dvh] = s_gi.n

            def cc_emit(g):
                g.collective_compute(
                    "AllGather", AO.bypass,
                    replica_groups=[[0, 1, 2, 3], [4, 5, 6, 7]],
                    ins=[agin.ap().opt()], outs=[agout.ap().opt()],
                ).then_inc(s_cc.h, 1)

            op("gpsimd", [(s_gi, gg_half[1])], cc_emit, [(s_cc, 1)])
            s_cc.inc()

            se_bias = 0
            for k in range(6):
                gw0 = [(s_mm, mm_attn_done)]
                if k > 0:
                    gw0.append((s_se, se_bias))
                op("gpsimd", gw0,
                   dma(stq[0:1, 0, :].bitcast(F32), bias_h[k:k + 1, :], s_bs), [(s_bs, 16)])
                s_bs.inc(16)
                bv = s_bs.n

                def b_emit(sc, k=k):
                    sc.copy(biasbf[0:1, k * 1024:(k + 1) * 1024],
                            stq[0:1, 0, :].bitcast(F32)).then_inc(s_se.h, 1)

                w3 = [(s_bs, bv)]
                if k == 0:
                    w3.append((s_ev, ev_hx1))
                op("scalar", w3, b_emit, [(s_se, 1)])
                se_bias = s_se.inc()

            dq_ev = {}
            dq_se = {}
            dq_gate = {"tensor": 0, "extra": []}

            def emit_dequant(k, slot, mm_gate, prev_k, extra_gw=()):
                for f in range(NF):
                    buf = f % 2
                    sq = s_q2[buf]
                    gw = [(s_mm, mm_gate)] + list(extra_gw)
                    if f >= 2:
                        gw += [(s_se, dq_se[(k, f - 2)]), (s_ev, dq_ev[(k, f - 2)])]
                    elif prev_k is not None:
                        gw += [(s_se, dq_se[(prev_k, NF - 2 + f)]),
                               (s_ev, dq_ev[(prev_k, NF - 2 + f)])]

                    def d_emit(g, k=k, f=f, buf=buf, sq=sq):
                        g.dma_start(out=stq[:, buf, :],
                                    in_=qw_h[k, f * 128:(f + 1) * 128, :]
                                    ).then_inc(sq.h, 16)
                        g.dma_start(out=sts[:, buf, :],
                                    in_=sr_h[k, f * 128:(f + 1) * 128, :]
                                    ).then_inc(sq.h, 16)

                    op("gpsimd", gw, d_emit, [(sq, 32)])
                    sq.inc(32)
                    qv = sq.n

                    def s_emit(sc, buf=buf):
                        sc.activation(dqtmp[:, buf, :], stq[:, buf, :], AF.Copy,
                                      bias=-1.0, scale=1.0 / 7.5).then_inc(s_se.h, 1)

                    op("scalar", [(sq, qv)], s_emit, [(s_se, 1)])
                    dq_se[(k, f)] = s_se.inc()

                    def v_emit(v, k=k, f=f, buf=buf, slot=slot, sq=sq, qv=qv):
                        v.tensor_tensor(wslot[slot][:, f, :], dqtmp[:, buf, :],
                                        sts[:, buf, :], AO.mult).then_inc(s_ev.h, 1)

                    w4 = [(s_se, dq_se[(k, f)]), (sq, qv)]
                    if k == 0 and f == 0:
                        w4.append((s_ev, ev_agu))
                    op("vector", w4, v_emit, [(s_ev, 1)])
                    dq_ev[(k, f)] = s_ev.inc()

            emit_dequant(0, 0, mm_attn_done, None, [(s_se, se_bias)])
            emit_dequant(1, 1, mm_attn_done, 0)

            ev_wsum = {}
            ev_sbf = [0, 0]
            for dvh in range(2):
                for r in range(4):
                    buf = r % 2
                    idx = dvh * 4 + r

                    def g_emit(g, r=r, dvh=dvh, buf=buf):
                        src = agout.ap()[r * D:(r + 1) * D, :] \
                            .rearrange("(f p) dv -> p f dv", p=128)
                        g.dma_start(out=gwin[buf][:],
                                    in_=src[:, :, dvh * 512:(dvh + 1) * 512]
                                    ).then_inc(s_g2[buf].h, 16)

                    gw = [(s_cc, 1)]
                    if idx >= 2:
                        gw.append((s_ev, ev_wsum[idx - 2]))
                    op("gpsimd", gw, g_emit, [(s_g2[buf], 16)])
                    s_g2[buf].inc(16)
                    gv = s_g2[buf].n

                    def w_emit(v, r=r, dvh=dvh, buf=buf):
                        for f in range(NF):
                            if r == 0:
                                q = v.tensor_scalar(sbf[dvh][:, f, :],
                                                    gwin[buf][:, f, :],
                                                    chw[:, 0:1], None, AO.mult)
                            else:
                                q = v.scalar_tensor_tensor(sbf[dvh][:, f, :],
                                                           gwin[buf][:, f, :],
                                                           chw[:, r:r + 1],
                                                           sbf[dvh][:, f, :],
                                                           AO.mult, AO.add)
                        q.then_inc(s_ev.h, 1)

                    op("vector", [(s_g2[buf], gv)], w_emit, [(s_ev, 1)])
                    ev_wsum[idx] = s_ev.inc()
                ev_sbf[dvh] = ev_wsum[dvh * 4 + 3]
            if upto == 'gather':
                return

            ev_hx2 = 0
            for dvh in range(2):
                for st0 in range(0, 16, 3):
                    tts = list(range(st0, min(st0 + 3, 16)))
                    waits = [(s_ev, ev_sbf[dvh])]
                    ems = []
                    for tt in tts:
                        def mk(tt=tt, dvh=dvh):
                            def mm(t, bt):
                                for f in range(NF):
                                    t.matmul(bt[:], qdec[:, f, tt * 128:(tt + 1) * 128],
                                             sbf[dvh][:, f, :],
                                             start=(f == 0), stop=(f == NF - 1))
                            return mm
                        ems.append(mk())
                    val, banks = mm_group(waits, ems)
                    for tt, bi in zip(tts, banks):
                        def ev_emit(v, tt=tt, dvh=dvh, bi=bi):
                            dst = hx[:, tt, dvh * 512:(dvh + 1) * 512]
                            v.tensor_add(dst, PSF[bi][:], dst).then_inc(s_ev.h, 1)

                        op("vector", [(s_mm, val)], ev_emit, [(s_ev, 1)])
                        ev_hx2 = s_ev.inc()
                        rot.release(bi, s_ev, ev_hx2)
            mm_cross_done = val

            SPC_SRC = None

            MEAN, VARO, INV, VEPS = 0, 64, 128, 192
            SA, SB, SC, YB, MB = 256, 320, 384, 448, 512

            def spacer(v):
                for _ in range(3):
                    v.tensor_scalar_mul(dumv[:, 0:64], qdk[:, 0:64], 1.0)

            def ln_stats_emit(v, idx, half0, half1):
                v.bn_stats(st6[:, idx, 0:6], half0)
                v.bn_stats(st6[:, idx, 6:12], half1)

            def lslab(off):
                return lnsc[:, off:off + 64]

            def ln_combine(v, idxs):
                sl = st6[:, 0:64, :]
                me = [sl[:, :, 1], sl[:, :, 4], sl[:, :, 7], sl[:, :, 10]]
                m2 = [sl[:, :, 2], sl[:, :, 5], sl[:, :, 8], sl[:, :, 11]]
                mean, var = lslab(MEAN), lslab(VARO)
                s1, s2, s3 = lslab(SA), lslab(SB), lslab(SC)
                g0, g1 = lslab(YB), lslab(MB)
                spacer(v)
                v.tensor_add(s1, me[0], me[1])
                v.tensor_add(s2, me[2], me[3])
                v.tensor_add(s3, s1, s2)
                v.tensor_scalar_mul(mean, s3, 0.25)
                v.tensor_mul(s1, me[0], me[0])
                v.tensor_mul(s2, me[1], me[1])
                v.tensor_add(g0, s1, s2)
                v.tensor_mul(s1, me[2], me[2])
                v.tensor_mul(s2, me[3], me[3])
                v.tensor_add(g1, s1, s2)
                v.tensor_add(s3, g0, g1)
                v.tensor_add(s1, m2[0], m2[1])
                v.tensor_add(s2, m2[2], m2[3])
                v.tensor_add(g0, s1, s2)
                v.tensor_scalar_mul(s1, g0, 1.0 / 1024.0)
                v.tensor_scalar_mul(s2, s3, 0.25)
                v.tensor_add(s3, s1, s2)
                v.tensor_mul(s1, mean, mean)
                v.tensor_sub(var, s3, s1)

            def ln_newton(v, n):
                var, inv, veps = lslab(VARO), lslab(INV), lslab(VEPS)
                sA, sB, sC, yB = lslab(SA), lslab(SB), lslab(SC), lslab(YB)
                v.tensor_scalar_add(veps, var, EPS)
                spacer(v)
                v.tensor_scalar(sA.bitcast(I32), veps.bitcast(I32), 1, None,
                                AO.arith_shift_right)
                spacer(v)
                v.tensor_sub(inv.bitcast(I32), magici[:], sA.bitcast(I32))
                spacer(v)
                ys = [inv, yB, inv, yB]
                for it in range(3):
                    y0, y1 = ys[it], ys[it + 1]
                    v.tensor_mul(sA, y0, y0)
                    v.tensor_mul(sB, veps, sA)
                    v.tensor_scalar(sC, sB, -0.5, 1.5, AO.mult, AO.add)
                    v.tensor_mul(y1, y0, sC)
                v.tensor_copy(inv, yB)
                mb = lslab(MB)
                return v.scalar_tensor_tensor(mb, lslab(MEAN), -1.0, inv,
                                              AO.mult, AO.mult)


            def ln1_stats(v):
                for tt in range(NTT):
                    v.tensor_copy(ln32[:, tt % 2, :], hx[:, tt, :])
                    ln_stats_emit(v, tt, ln32[:, tt % 2, 0:512],
                                  ln32[:, tt % 2, 512:1024])
                ln_combine(v, range(NTT))
                ln_newton(v, NTT).then_inc(s_ev.h, 1)

            op("vector", [(s_ev, ev_hx2), (s_ev, ev_misc)], ln1_stats, [(s_ev, 1)])
            ev_ln1 = s_ev.inc()

            def ln1_apply(sc):
                for tt in range(NTT):
                    sc.activation(hF[:, tt, :], hx[:, tt, :], AF.Identity,
                                  bias=lnsc[:, MB + tt:MB + tt + 1],
                                  scale=lnsc[:, INV + tt:INV + tt + 1])
                for tt in range(NTT):
                    q = sc.activation(hx[:, tt, :], hx[:, tt, :], AF.Identity,
                                      bias=lnsc[:, MB + tt:MB + tt + 1],
                                      scale=lnsc[:, INV + tt:INV + tt + 1])
                q.then_inc(s_se.h, 1)

            op("scalar", [(s_ev, ev_ln1), (s_mm, mm_attn_done)], ln1_apply, [(s_se, 1)])
            se_ln1 = s_se.inc()

            flush()
            if upto == 'phase2':
                return

            ev_h_final = 0
            prev_y2_val = mm_cross_done
            se_lb = 0
            for b in range(3):
                k1, k2 = 2 * b, 2 * b + 1
                sl1, sl2 = k1 % 3, k2 % 3
                w1m, w2m = wslot[sl1], wslot[sl2]

                lbw = [(s_se, dq_se[(k1, NF - 1)]), (s_se, dq_se[(k2, NF - 1)]),
                       (s_ev, dq_ev[(k1, NF - 1)]), (s_ev, dq_ev[(k2, NF - 1)]),
                       (s_se, dq_se[(k2, NF - 2)]), (s_ev, dq_ev[(k2, NF - 2)]),
                       (s_se, se_bias)]
                def lb_dma(g, b=b):
                    g.dma_start(out=lbst[:, 0, :], in_=lb_h[2 * b]).then_inc(s_lb.h, 16)
                    g.dma_start(out=lbst[:, 1, :], in_=lb_h[2 * b + 1]).then_inc(s_lb.h, 16)

                op("gpsimd", lbw, lb_dma, [(s_lb, 32)])
                s_lb.inc(32)
                lbv = s_lb.n

                def lb_cast(sc):
                    sc.copy(lbbf[:].rearrange("p a b -> p (a b)"),
                            lbst[:].rearrange("p a b -> p (a b)")
                            ).then_inc(s_se.h, 1)

                op("scalar", [(s_lb, lbv)], lb_cast, [(s_se, 1)])
                se_lb = s_se.inc()

                for g in range(4):
                    tts = [g * 4 + i for i in range(4)]
                    hw = [(s_se, se_ln1)] if b == 0 else [(s_ev, ev_h_final)]
                    ev_hfm_last = 0
                    for f in range(NF):
                        rounds = [(hF[:, tt, f * 128:(f + 1) * 128], i * 128)
                                  for i, tt in enumerate(tts)]
                        val, bi, bt = tr_group(list(hw), rounds)

                        def ev_emit(v, f=f, bt=bt):
                            v.tensor_copy(hfm[:, f, :], bt[:, 0:512]).then_inc(s_ev.h, 1)

                        op("vector", [(s_mm, val)], ev_emit, [(s_ev, 1)])
                        ev_hfm_last = s_ev.inc()
                        rott.release(bi, s_ev, ev_hfm_last)

                    def lora1(t, bt, k1=k1):
                        for f in range(NF):
                            t.matmul(bt[0:R, :],
                                     labf[:, (k1 * NF + f) * R:(k1 * NF + f + 1) * R],
                                     hfm[:, f, :], start=(f == 0), stop=(f == NF - 1))

                    val, banks = mm_group([(s_ev, ev_hfm_last), (s_ev, ev_la)], [lora1])
                    bi = banks[0]

                    def t1_ev(v, bi=bi):
                        v.tensor_copy(t1bf[:, 0, :], PSF[bi][0:R, :]).then_inc(s_ev.h, 1)

                    op("vector", [(s_mm, val)], t1_ev, [(s_ev, 1)])
                    ev_t1 = s_ev.inc()
                    rot.release(bi, s_ev, ev_t1)

                    ev_y1 = 0
                    for st0 in range(0, 8, 3):
                        items = [(i // 2, i % 2) for i in range(st0, min(st0 + 3, 8))]
                        waits = [(s_ev, ev_t1), (s_ev, dq_ev[(k1, NF - 1)]),
                                 (s_se, se_lb), (s_se, se_bias)]
                        ems = []
                        for tl, oh in items:
                            def mk(tl=tl, oh=oh, k1=k1, w1m=w1m):
                                def mm(t, bt):
                                    for f in range(NF):
                                        t.matmul(bt[:], hfm[:, f, tl * 128:(tl + 1) * 128],
                                                 w1m[:, f, oh * 512:(oh + 1) * 512],
                                                 start=(f == 0), stop=False)
                                    t.matmul(bt[:], ones[0:1, 0:128],
                                             biasbf[0:1,
                                                    k1 * 1024 + oh * 512:
                                                    k1 * 1024 + (oh + 1) * 512],
                                             start=False, stop=False)
                                    t.matmul(bt[:], t1bf[:, 0, tl * 128:(tl + 1) * 128],
                                             lbbf[:, 0, oh * 512:(oh + 1) * 512],
                                             start=False, stop=True)
                                return mm
                            ems.append(mk())
                        val, banks = mm_group(waits, ems)
                        for (tl, oh), bi in zip(items, banks):
                            def ev_emit(v, tl=tl, oh=oh, bi=bi):
                                v.bn_stats(st6[:, tl, 6 * oh:6 * oh + 6], PSF[bi][:])
                                v.tensor_copy(y1bf[:, tl, oh * 512:(oh + 1) * 512],
                                              PSF[bi][:]).then_inc(s_ev.h, 1)

                            op("vector", [(s_mm, val)], ev_emit, [(s_ev, 1)])
                            ev_y1 = s_ev.inc()
                            rot.release(bi, s_ev, ev_y1)

                    def st_emit(v):
                        ln_combine(v, range(4))
                        ln_newton(v, 4).then_inc(s_ev.h, 1)

                    op("vector", [(s_ev, ev_y1)], st_emit, [(s_ev, 1)])
                    ev_st = s_ev.inc()

                    def ap_emit(sc):
                        for i in range(4):
                            q = sc.activation(tmpbf[:, i, :], y1bf[:, i, :],
                                              AF.Identity,
                                              bias=lnsc[:, MB + i:MB + i + 1],
                                              scale=lnsc[:, INV + i:INV + i + 1])
                        q.then_inc(s_se.h, 1)

                    op("scalar", [(s_ev, ev_st)], ap_emit, [(s_se, 1)])
                    se_tmp = s_se.inc()

                    se_yfm = 0
                    for f in range(NF):
                        rounds = [(tmpbf[:, i, f * 128:(f + 1) * 128], i * 128)
                                  for i in range(4)]
                        val, bi, bt = tr_group([(s_se, se_tmp)], rounds)

                        def g_emit2(sc, f=f, bt=bt, b=b):
                            sc.activation(yfm[:, f, :], bt[:, 0:512], AF.Gelu,
                                          bias=bfm[:, b, f:f + 1],
                                          scale=gfm[:, b, f:f + 1]).then_inc(s_se.h, 1)

                        op("scalar", [(s_mm, val)], g_emit2, [(s_se, 1)])
                        se_yfm = s_se.inc()
                        rott.release(bi, s_se, se_yfm)

                    def lora2(t, bt, k2=k2):
                        for f in range(NF):
                            t.matmul(bt[0:R, :],
                                     labf[:, (k2 * NF + f) * R:(k2 * NF + f + 1) * R],
                                     yfm[:, f, :], start=(f == 0), stop=(f == NF - 1))

                    val, banks = mm_group([(s_se, se_yfm)], [lora2])
                    bi = banks[0]

                    def t1b_ev(v, bi=bi):
                        v.tensor_copy(t1bf[:, 1, :], PSF[bi][0:R, :]).then_inc(s_ev.h, 1)

                    op("vector", [(s_mm, val)], t1b_ev, [(s_ev, 1)])
                    ev_t1b = s_ev.inc()
                    rot.release(bi, s_ev, ev_t1b)

                    for st0 in range(0, 8, 3):
                        items = [(i // 2, i % 2) for i in range(st0, min(st0 + 3, 8))]
                        waits = [(s_ev, ev_t1b), (s_ev, dq_ev[(k2, NF - 1)]),
                                 (s_se, se_yfm)]
                        ems = []
                        for tl, oh in items:
                            def mk(tl=tl, oh=oh, k2=k2, w2m=w2m):
                                def mm(t, bt):
                                    for f in range(NF):
                                        t.matmul(bt[:], yfm[:, f, tl * 128:(tl + 1) * 128],
                                                 w2m[:, f, oh * 512:(oh + 1) * 512],
                                                 start=(f == 0), stop=False)
                                    t.matmul(bt[:], ones[0:1, 0:128],
                                             biasbf[0:1,
                                                    k2 * 1024 + oh * 512:
                                                    k2 * 1024 + (oh + 1) * 512],
                                             start=False, stop=False)
                                    t.matmul(bt[:], t1bf[:, 1, tl * 128:(tl + 1) * 128],
                                             lbbf[:, 1, oh * 512:(oh + 1) * 512],
                                             start=False, stop=True)
                                return mm
                            ems.append(mk())
                        val, banks = mm_group(waits, ems)
                        for (tl, oh), bi in zip(items, banks):
                            def ev_emit(v, tl=tl, oh=oh, bi=bi, tts=tts):
                                dst = hF[:, tts[tl], oh * 512:(oh + 1) * 512]
                                v.tensor_add(dst, PSF[bi][:], dst).then_inc(s_ev.h, 1)

                            op("vector", [(s_mm, val)], ev_emit, [(s_ev, 1)])
                            ev_h_final = s_ev.inc()
                            rot.release(bi, s_ev, ev_h_final)
                    prev_y2_val = val

                if b < 2:
                    emit_dequant(2 * b + 2, (2 * b + 2) % 3, prev_y2_val, 2 * b + 1,
                                 [(s_se, se_lb)])
                    emit_dequant(2 * b + 3, (2 * b + 3) % 3, prev_y2_val, 2 * b + 2)

                flush()
                if upto == f'ffn{b}':
                    return

            def sumsrc(tt):
                return sumalt[:, tt - 13, :] if tt in (13, 14) else sumall[:, tt, :]

            def sum_emit(v):
                for tt in range(NTT):
                    v.tensor_add(sumsrc(tt), hx[:, tt, :], hF[:, tt, :])
                for tt in range(NTT):
                    ln_stats_emit(v, tt, sumsrc(tt)[:, 0:512],
                                  sumsrc(tt)[:, 512:1024])
                ln_combine(v, range(NTT))
                ln_newton(v, NTT).then_inc(s_ev.h, 1)

            op("vector", [(s_ev, ev_h_final), (s_mm, prev_y2_val)], sum_emit, [(s_ev, 1)])
            ev_sum = s_ev.inc()

            for tt in range(NTT):
                buf = tt % 2

                def o_emit(sc, tt=tt, buf=buf):
                    sc.activation(out32[:, buf, :], sumsrc(tt), AF.Identity,
                                  bias=lnsc[:, MB + tt:MB + tt + 1],
                                  scale=lnsc[:, INV + tt:INV + tt + 1]).then_inc(s_se.h, 1)

                w5 = [(s_ev, ev_sum)]
                if tt >= 2:
                    w5.append((s_st2[buf], (tt - 2) // 2 * 16 + 16))
                op("scalar", w5, o_emit, [(s_se, 1)])
                sev = s_se.inc()

                def d_emit(sy, tt=tt, buf=buf, sev=sev):
                    sy.dma_start(out=out_h[tt * 128:(tt + 1) * 128, :],
                                 in_=out32[:, buf, :]).then_inc(s_st2[buf].h, 16)

                op("sync", [(s_se, sev)], d_emit, [(s_st2[buf], 16)])
                s_st2[buf].inc(16)

            op("sync", [(s_st2[0], s_st2[0].n), (s_st2[1], s_st2[1].n)],
               lambda sy: None)
            flush()
        emit_all()
        emit_taps()
        op('sync', [(s_st2[0], s_st2[0].n), (s_st2[1], s_st2[1].n)],
           lambda sy: None)
        flush()


        queues = {}
        for i, (eng, waits, incs) in enumerate(all_ops):
            queues.setdefault(eng, []).append((i, waits, incs))
        heads = {e: 0 for e in queues}
        totals = {}
        progress = True
        while progress:
            progress = False
            for e, q in queues.items():
                while heads[e] < len(q):
                    i, waits, incs = q[heads[e]]
                    if all(totals.get(s.h.num, 0) >= v for s, v in waits):
                        for s, v in incs:
                            totals[s.h.num] = totals.get(s.h.num, 0) + v
                        heads[e] += 1
                        progress = True
                    else:
                        break
        stuck = {e: len(q) - heads[e] for e, q in queues.items()
                 if heads[e] < len(q)}
        if stuck:
            details = []
            for e, q in queues.items():
                if heads[e] < len(q):
                    i, waits, _ = q[heads[e]]
                    bad = [(s.h.num, v, totals.get(s.h.num, 0))
                           for s, v in waits if totals.get(s.h.num, 0) < v]
                    details.append(f"{e}@op{i}: needs {bad}")
            raise RuntimeError(f"schedule deadlock: {stuck}; {details}")

    assert nc.sbuf_base <= A0, f"bump allocator collided: {nc.sbuf_base}"
    return nc


_NC_CACHE = {}


def _get_nc():
    if "nc" not in _NC_CACHE:
        _NC_CACHE["nc"] = build_program()
    return _NC_CACHE["nc"]


def host_inputs(L0_attn_x, L0_attn_W_qkv, L0_attn_b_qkv, L0_ffn_qweights,
                L0_ffn_scales, L0_ffn_biases, L0_ffn_lora_a, L0_ffn_lora_b,
                L0_ffn_ln_gamma, L0_ffn_ln_beta):
    bf = ml_dtypes.bfloat16
    x = np.ascontiguousarray(np.asarray(L0_attn_x, dtype=np.float32))
    wqkvT = np.ascontiguousarray(np.asarray(L0_attn_W_qkv, np.float32).T)
    bqkv = np.ascontiguousarray(np.asarray(L0_attn_b_qkv, np.float32)[None, :])
    qwT = np.ascontiguousarray(
        np.asarray(L0_ffn_qweights, np.int32).transpose(0, 2, 1))
    srepT = np.ascontiguousarray(
        np.repeat(np.asarray(L0_ffn_scales, np.float32), 16, axis=2)
        .transpose(0, 2, 1))
    biases = np.ascontiguousarray(np.asarray(L0_ffn_biases, np.float32))
    la = np.asarray(L0_ffn_lora_a, np.float32)
    laT_r = np.ascontiguousarray(
        la.reshape(6, R, NF, 128).transpose(3, 0, 2, 1).reshape(128, 6 * NF * R))
    lbT = np.ascontiguousarray(np.asarray(L0_ffn_lora_b, np.float32)
                               .transpose(0, 2, 1))
    gamma_fm = np.ascontiguousarray(
        np.asarray(L0_ffn_ln_gamma, np.float32).reshape(3, NF, 128)
        .transpose(2, 0, 1))
    beta_fm = np.ascontiguousarray(
        np.asarray(L0_ffn_ln_beta, np.float32).reshape(3, NF, 128)
        .transpose(2, 0, 1))

    ident = np.eye(128, dtype=bf)
    ones_bf = np.ones((1, 512), dtype=bf)
    ii = np.arange(C)[None, None, :]
    jj = (np.arange(4)[None, :, None] * 128 + np.arange(128)[:, None, None])
    causal = np.ascontiguousarray((ii >= jj).astype(bf))
    t_loc = np.arange(T, dtype=np.float64)
    qdecay = np.ascontiguousarray(
        np.broadcast_to((ALPHA ** t_loc / 32.0).astype(np.float32), (128, T)))
    kdecay = np.ascontiguousarray(
        (ALPHA ** (-(np.arange(NTT)[None, :] * 128
                     + np.arange(128)[:, None]).astype(np.float64))
         ).astype(np.float32))

    in_maps = []
    for core in range(NCORES):
        b, j = core // 4, core % 4
        chwv = np.zeros((4,), np.float64)
        for cprev in range(4):
            if cprev < j:
                chwv[cprev] = ALPHA ** (2048.0 * (j - cprev))
        chainw = np.ascontiguousarray(
            np.broadcast_to(chwv.astype(np.float32), (128, 4)))
        in_maps.append({
            "x": np.ascontiguousarray(x[b, j * T:(j + 1) * T, :]),
            "wqkvT": wqkvT, "bqkv": bqkv, "qwT": qwT, "srepT": srepT,
            "biases": biases, "laT_r": laT_r, "lbT": lbT,
            "gamma_fm": gamma_fm, "beta_fm": beta_fm, "chainw": chainw,
            "ident": ident, "ones_bf": ones_bf, "causal": causal,
            "qdecay": qdecay, "kdecay": kdecay,
        })
    return in_maps


def kernel(**inputs):
    in_maps = host_inputs(**inputs)
    nc = _get_nc()
    res = run_bass_kernel_spmd(nc, in_maps, core_ids=list(range(NCORES)))
    out = np.empty((B, S, D), np.float32)
    for core in range(NCORES):
        b, j = core // 4, core % 4
        out[b, j * T:(j + 1) * T, :] = res.results[core]["out"]
    return out
